# version 01
# baseline (speedup 1.0000x reference)
import numpy as np
import ml_dtypes
import concourse.bass as bass
import concourse.mybir as mybir
from concourse.bass_utils import run_bass_kernel_spmd

F32 = mybir.dt.float32
BF16 = mybir.dt.bfloat16
F16 = mybir.dt.float16
U8 = mybir.dt.uint8
I32 = mybir.dt.int32
AF = mybir.ActivationFunctionType
ALU = mybir.AluOpType
AX = mybir.AxisListType

NCORE = 8
BATCH = 4
SEQ = 2048
DM = 2048
TOK = 1024
NH = 16
DK = 64
C = 64
BF = ml_dtypes.bfloat16

ENGINES = ("tensor", "vector", "scalar", "gpsimd", "sync")


class Prog:
    def __init__(self, dma_scratch=16384):
        self.nc = bass.Bass("TRN2", target_bir_lowering=False, dynamic_dma_scratch_size=dma_scratch)
        self.ops = {e: [] for e in ENGINES}
        self.counts = {}
        self._tensors = {}
        self.finalized = False

    def din(self, name, shape, dt):
        return self.nc.dram_tensor(name, list(shape), dt, kind="ExternalInput").ap()

    def dout(self, name, shape, dt):
        return self.nc.dram_tensor(name, list(shape), dt, kind="ExternalOutput").ap()

    def dtmp(self, name, shape, dt):
        return self.nc.dram_tensor(name, list(shape), dt, kind="Internal").ap()

    def sb(self, name, shape, dt):
        return self.nc.alloc_sbuf_tensor(name, list(shape), dt)

    def ps(self, name, shape, dt=F32):
        return self.nc.alloc_psum_tensor(name, list(shape), dt)

    DMA_POOL = 6

    def _tok(self, engine, kind):
        key = (engine, kind)
        n = self.counts.get(key, 0) + (16 if kind[0] == "d" else 1)
        self.counts[key] = n
        return (key, n)

    def op(self, engine, emit, after=()):
        waits = self._norm(after)
        tok = self._tok(engine, "c")
        self.ops[engine].append((emit, waits, tok))
        return tok

    def dma(self, queue, out, in_, after=(), transpose=False, **kw):
        waits = self._norm(after)
        idx = self.counts.get((queue, "di"), 0)
        self.counts[(queue, "di")] = idx + 1
        kind = "d%d" % (idx % self.DMA_POOL)
        key = (queue, kind)
        prev = self.counts.get(key, 0)
        if prev:
            waits = [(key, prev)] + waits
        tok = (key, prev + 16)
        self.counts[key] = prev + 16
        if transpose:
            emit = lambda eng: eng.dma_start_transpose(out=out, in_=in_)
        else:
            emit = lambda eng: eng.dma_start(out=out, in_=in_, **kw)
        self.ops[queue].append((emit, waits, tok))
        return tok

    def _norm(self, after):
        if after is None:
            return []
        if isinstance(after, tuple) and len(after) == 2 and isinstance(after[0], tuple):
            return [after]
        best = {}
        for a in after:
            if a is None:
                continue
            k, v = a
            if best.get(k, -1) < v:
                best[k] = v
        return list(best.items())

    def barrier_tokens(self):
        return [(k, v) for k, v in self.counts.items() if k[1] != "di"]

    def realize(self):
        assert not self.finalized
        self.finalized = True
        nc = self.nc
        from contextlib import ExitStack
        es = ExitStack()
        sems = {}
        for key in self.counts:
            if key[1] == "di":
                continue
            sems[key] = es.enter_context(nc.semaphore(f"{key[0]}_{key[1]}"))
        block = es.enter_context(nc.Block())
        for ename in ENGINES:
            ops = self.ops[ename]
            if not ops:
                continue

            def body(eng, ops=ops):
                for emit, waits, tok in ops:
                    for (wk, wv) in waits:
                        eng.wait_ge(sems[wk], wv)
                    inst = emit(eng)
                    inst.then_inc(sems[tok[0]], 16 if tok[0][1].startswith("d") else 1)

            getattr(block, ename)(body)
        es.close()
        return nc

    def load_const_f32(self, name, arr):
        arr = np.ascontiguousarray(arr, dtype=np.float32)
        d = self.din(name, arr.shape, F32)
        s = self.sb(name + "_sb", arr.shape, F32)
        t = self.dma("sync", s[:, :], d)
        return s, t, arr


def bfh(x):
    return np.ascontiguousarray(x, dtype=BF)


def f32h(x):
    return np.ascontiguousarray(x, dtype=np.float32)


def build_L0a(groups=("gup", "d0", "up", "dn")):
    p = Prog()
    gup_pk = p.din("gup_pk", [1408, 256], I32)
    gup_sc = p.din("gup_sc", [1408, 16], F32)
    d0_pk = p.din("d0_pk", [256, 704], I32)
    d0_sc = p.din("d0_sc", [256, 44], F32)
    up_q = p.din("up_q", [1024, 2048], I32)
    up_sc = p.din("up_sc", [1024, 32], F32)
    dn_q = p.din("dn_q", [256, 8192], I32)
    dn_sc = p.din("dn_sc", [256, 128], F32)

    gup_deq = p.dout("gup_deq", [1408, 2048], BF16)
    d0_deq = p.dout("d0_deq", [256, 5632], BF16)
    up_deq = p.dout("up_deq", [1024, 2048], F16)
    dn_deq = p.dout("dn_deq", [256, 8192], F16)

    NBMAX = 2816
    pk_sb = p.sb("pk_sb", [128, NBMAX // 4], I32)
    scf = p.sb("scf", [128, 128], F32)
    t_u8 = p.sb("t_u8", [128, NBMAX], U8)
    s_u8 = p.sb("s_u8", [128, NBMAX], U8)
    t_f = p.sb("t_f", [128, NBMAX], F32)
    s_f = p.sb("s_f", [128, NBMAX], F32)
    r1 = p.sb("r1", [128, NBMAX], F32)
    r2 = p.sb("r2", [128, NBMAX], F32)
    valb = p.sb("valb", [128, NBMAX], F32)
    ob = p.sb("ob", [128, NBMAX], BF16)
    q_sb = p.sb("q_sb", [128, 8192], I32)
    qf = p.sb("qf", [128, 8192], F32)
    ob2 = p.sb("ob2", [128, 8192], F16)

    state = {"oo": None, "st": None, "cv": None}

    def fp4_tiles(tag, pk, sc, n_tiles, W, G, out):
        NB = 4 * W
        scv = scf[:, 0:G].unsqueeze(2).broadcast_to([128, G, 64])
        for it in range(n_tiles):
            rows = slice(it * 128, it * 128 + 128)
            tl = p.dma("sync", pk_sb[:, 0:W], pk[rows, :], after=[state["oo"], state["cv"]])
            tc = p.dma("sync", scf[:, 0:G], sc[rows, :], after=[state["oo"], state["cv"]])
            u8 = pk_sb[:, 0:W].bitcast(U8)
            for plane in range(2):
                cols = slice(plane * NB, plane * NB + NB)
                po = state["oo"]
                if plane == 0:
                    e1 = p.op("vector", lambda v, u8=u8, NB=NB: v.tensor_scalar(t_u8[:, 0:NB], u8, 7, None, ALU.bitwise_and), after=[tl, po])
                    e2 = p.op("vector", lambda v, u8=u8, NB=NB: v.tensor_scalar(s_u8[:, 0:NB], u8, 8, None, ALU.bitwise_and), after=[tl, po])
                    sf_mul = -0.25
                else:
                    e1 = p.op("vector", lambda v, u8=u8, NB=NB: v.tensor_scalar(t_u8[:, 0:NB], u8, 4, 7, ALU.logical_shift_right, ALU.bitwise_and), after=[tl, po])
                    e2 = p.op("vector", lambda v, u8=u8, NB=NB: v.tensor_scalar(s_u8[:, 0:NB], u8, 7, None, ALU.logical_shift_right), after=[tl, po])
                    sf_mul = -2.0
                c1 = p.op("scalar", lambda s, NB=NB: s.copy(t_f[:, 0:NB], t_u8[:, 0:NB]), after=[e1])
                c2 = p.op("scalar", lambda s, NB=NB: s.copy(s_f[:, 0:NB], s_u8[:, 0:NB]), after=[e2])
                sf = p.op("vector", lambda v, m=sf_mul, NB=NB: v.tensor_scalar(s_f[:, 0:NB], s_f[:, 0:NB], m, 1.0, ALU.mult, ALU.add), after=[c2])
                g1 = p.op("vector", lambda v, NB=NB: v.tensor_scalar(r1[:, 0:NB], t_f[:, 0:NB], -4.0, 0.0, ALU.add, ALU.max), after=[c1])
                g2 = p.op("vector", lambda v, NB=NB: v.tensor_scalar(r2[:, 0:NB], t_f[:, 0:NB], -6.0, 0.0, ALU.add, ALU.max), after=[c1])
                aa = p.op("vector", lambda v, NB=NB: v.tensor_add(r1[:, 0:NB], t_f[:, 0:NB], r1[:, 0:NB]), after=[c1, g1])
                mm = p.op("vector", lambda v, NB=NB: v.scalar_tensor_tensor(r2[:, 0:NB], r1[:, 0:NB], 0.5, r2[:, 0:NB], ALU.mult, ALU.add), after=[aa, g2])
                vv = p.op("vector", lambda v, NB=NB: v.tensor_mul(valb[:, 0:NB], r2[:, 0:NB], s_f[:, 0:NB]), after=[mm, sf])
                oo = p.op("vector", lambda v, NB=NB, G=G, scv=scv: v.tensor_tensor(
                    ob[:, 0:NB], valb[:, 0:NB].rearrange("p (g x) -> p g x", g=G), scv, ALU.mult),
                    after=[vv, tc, state["st"]])
                st = p.dma("sync", out[rows, cols], ob[:, 0:NB], after=[oo])
                state["oo"], state["st"] = oo, st

    if "gup" in groups:
        fp4_tiles("gup", gup_pk, gup_sc, 11, 256, 16, gup_deq)
    if "d0" in groups:
        fp4_tiles("d0", d0_pk, d0_sc, 2, 704, 44, d0_deq)

    def q4_tiles(tag, q, sc, n_tiles, W, G, out):
        scv = scf[:, 0:G].unsqueeze(2).broadcast_to([128, G, 64])
        for it in range(n_tiles):
            rows = slice(it * 128, it * 128 + 128)
            tl = p.dma("sync", q_sb[:, 0:W], q[rows, :], after=[state["oo"], state["cv"]])
            ts = p.dma("sync", scf[:, 0:G], sc[rows, :], after=[state["oo"], state["cv"]])
            cv = p.op("scalar", lambda s, W=W: s.copy(qf[:, 0:W], q_sb[:, 0:W]), after=[tl, state["cv"]])
            oo = p.op("vector", lambda v, W=W, G=G, scv=scv: v.scalar_tensor_tensor(
                ob2[:, 0:W].rearrange("p (g x) -> p g x", g=G), qf[:, 0:W].rearrange("p (g x) -> p g x", g=G),
                -7.5, scv, ALU.add, ALU.mult), after=[cv, ts, state["st"], state["oo"]])
            st = p.dma("sync", out[rows, :], ob2[:, 0:W], after=[oo])
            state["oo"], state["st"], state["cv"] = oo, st, cv

    if "up" in groups:
        q4_tiles("up", up_q, up_sc, 8, 2048, 32, up_deq)
    if "dn" in groups:
        q4_tiles("dn", dn_q, dn_sc, 2, 8192, 128, dn_deq)

    toks = p.barrier_tokens()
    p.op("sync", lambda s: s.nop(), after=toks)
    return p.realize()


_L0A_CACHE = {}


def run_L0a(inp, groups=("gup", "d0", "up", "dn")):
    nc = build_L0a(groups)
    in_maps = []
    for c in range(NCORE):
        in_maps.append(dict(
            gup_pk=np.ascontiguousarray(inp["L0_ffn_gate_up_packed"][c * 1408:(c + 1) * 1408]),
            gup_sc=np.ascontiguousarray(inp["L0_ffn_gate_up_scales"][:, c * 1408:(c + 1) * 1408].T.astype(np.float32)),
            d0_pk=np.ascontiguousarray(inp["L0_ffn_down_packed"][c * 256:(c + 1) * 256]),
            d0_sc=np.ascontiguousarray(inp["L0_ffn_down_scales"][:, c * 256:(c + 1) * 256].T.astype(np.float32)),
            up_q=np.ascontiguousarray(inp["L1_ffn_w_up_q"][c * 1024:(c + 1) * 1024]),
            up_sc=np.ascontiguousarray(inp["L1_ffn_w_up_scale"][c * 1024:(c + 1) * 1024]),
            dn_q=np.ascontiguousarray(inp["L1_ffn_w_down_q"][c * 256:(c + 1) * 256]),
            dn_sc=np.ascontiguousarray(inp["L1_ffn_w_down_scale"][c * 256:(c + 1) * 256]),
        ))
    res = run_bass_kernel_spmd(nc, in_maps, core_ids=list(range(NCORE)))

    def deinterleave(parts):
        cat = np.concatenate([np.asarray(r) for r in parts], axis=0)
        P = cat.shape[1] // 2
        out = np.empty((cat.shape[0], 2 * P), dtype=cat.dtype)
        out[:, 0::2] = cat[:, :P]
        out[:, 1::2] = cat[:, P:]
        return out

    W_gu = deinterleave([res.results[c]["gup_deq"] for c in range(NCORE)])
    W_d0 = deinterleave([res.results[c]["d0_deq"] for c in range(NCORE)])
    W_up = np.concatenate([np.asarray(res.results[c]["up_deq"]) for c in range(NCORE)], axis=0)
    W_dn = np.concatenate([np.asarray(res.results[c]["dn_deq"]) for c in range(NCORE)], axis=0)
    return dict(
        W_guT=np.ascontiguousarray(W_gu.T),
        W_d0T=np.ascontiguousarray(W_d0.T),
        W_upT=np.ascontiguousarray(W_up.T),
        W_dnT=np.ascontiguousarray(W_dn.T),
    )


def emit_gemm(p, lhsT_of, rhs_of, psums, n_k, n_m, n_n, NSL, evac, deps):
    evac_toks = {}
    for mi in range(n_m):
        psd = psums[mi % 2]
        aft = list(deps) + list(evac_toks.get(mi - 2, []))
        last = None
        for k in range(n_k):
            for nj in range(n_n):
                last = p.op("tensor", lambda t, k=k, mi=mi, nj=nj, psd=psd: t.matmul(
                    psd[:, nj * NSL:(nj + 1) * NSL], lhsT_of(k, mi), rhs_of(k, nj),
                    start=(k == 0), stop=(k == n_k - 1)), after=aft)
                aft = []
        evac_toks[mi] = evac(mi, psd, last)
    out = []
    for v in evac_toks.values():
        out += list(v)
    return out


def emit_ln(p, x_ap, nrow, out_bf_ap, scratch, deps, eps=1e-5, width=DM):
    sq, red = scratch["sq"], scratch["red"]
    r = lambda a: a[0:nrow, :]
    c = lambda a, i: a[0:nrow, i:i + 1]
    t1 = p.op("vector", lambda v: v.reduce_sum(c(red, 0), x_ap, AX.X), after=deps)
    t2 = p.op("scalar", lambda s: s.activation(sq[0:nrow, 0:width], x_ap, AF.Square,
                                               accum_out=c(red, 1)), after=deps)
    t3 = p.op("vector", lambda v: v.tensor_scalar(c(red, 0), c(red, 0), 1.0 / width, None, ALU.mult), after=[t1])
    t4 = p.op("vector", lambda v: v.tensor_scalar(c(red, 1), c(red, 1), 1.0 / width, None, ALU.mult), after=[t2])
    t5 = p.op("vector", lambda v: v.tensor_scalar(c(red, 2), c(red, 0), c(red, 0), None, ALU.mult), after=[t3])
    t6 = p.op("vector", lambda v: v.tensor_tensor(c(red, 2), c(red, 1), c(red, 2), ALU.subtract), after=[t4, t5])
    t7 = p.op("vector", lambda v: v.tensor_scalar(c(red, 2), c(red, 2), eps, None, ALU.add), after=[t6])
    t8 = p.op("scalar", lambda s: s.activation(c(red, 3), c(red, 2), AF.Sqrt), after=[t7])
    t9 = p.op("vector", lambda v: v.reciprocal(c(red, 3), c(red, 3)), after=[t8])
    t10 = p.op("vector", lambda v: v.tensor_scalar(out_bf_ap, x_ap, c(red, 0), c(red, 3),
                                                   ALU.subtract, ALU.mult), after=[t9, t3])
    return t10


def build_L1():
    p = Prog()
    x_ext = p.din("x_ext", [1025, DM], F32)
    Wx1 = p.din("Wx1", [DM, 320], F32)
    Wx2n = p.din("Wx2n", [5, 64, DM], F32)
    Wr = p.din("Wr", [DM, 1024], F32)
    Wk = p.din("Wk", [DM, 1024], F32)
    Wv = p.din("Wv", [DM, 1024], F32)
    Wwa = p.din("Wwa", [DM, 64], F32)
    Wwb = p.din("Wwb", [64, 1024], F32)
    Wga = p.din("Wga", [DM, 64], F32)
    Wgb = p.din("Wgb", [64, 1024], F32)
    mu_cols = p.din("mu_cols", [128, 16], F32)
    xb_cols = p.din("xb_cols", [128, 80], F32)
    bw_rep = p.din("bw_rep", [128, 1024], F32)
    bg_rep = p.din("bg_rep", [128, 1024], F32)
    ident_in = p.din("ident", [128, 128], BF16)

    xn_out = p.dout("xn", [1025, DM], BF16)
    r_out = p.dout("r", [TOK, 1024], F32)
    k_out = p.dout("k", [TOK, 1024], F32)
    v_out = p.dout("v", [TOK, 1024], F32)
    ew_out = p.dout("expw", [TOK, 1024], F32)
    g_out = p.dout("g", [TOK, 1024], F32)

    big_a = p.sb("big_a", [128, 9 * DM], BF16)
    xnT = [p.sb(f"xnT{j}", [128, 1025], BF16) for j in range(16)]
    dlt = p.sb("dlt", [128, 1024], BF16)
    lr_n = [p.sb(f"lr_n{i}", [64, 1024], BF16) for i in range(5)]
    wx1_bf = p.sb("wx1_bf", [128, 16 * 320], BF16)
    wx2_bf = p.sb("wx2_bf", [64, 5 * DM], BF16)
    wbuf0 = p.sb("wbuf0", [128, 16 * 1024], BF16)
    wwa_bf = p.sb("wwa_bf", [128, 16 * 64], BF16)
    wwb_bf = p.sb("wwb_bf", [64, 1024], BF16)
    wga_bf = p.sb("wga_bf", [128, 16 * 64], BF16)
    wgb_bf = p.sb("wgb_bf", [64, 1024], BF16)
    mu_sb = p.sb("mu_sb", [128, 16], F32)
    xb_sb = p.sb("xb_sb", [128, 80], F32)
    bwr = p.sb("bwr", [128, 1024], BF16)
    bgr = p.sb("bgr", [128, 1024], BF16)
    ident = p.sb("ident_sb", [128, 128], BF16)
    x_tile = p.sb("x_tile", [128, DM], F32)
    sq = p.sb("sq", [128, DM], F32)
    red = p.sb("red", [128, 4], F32)
    tz = p.sb("tz", [64, 1024], BF16)
    ev = [p.sb(f"ev{i}", [128, 1024], F32) for i in range(2)]
    ev2 = sq
    mt = p.sb("mt", [128, 1024], BF16)

    psA = p.ps("psA", [128, 1024])
    psB = p.ps("psB", [128, 1024])
    psM = p.ps("psM", [128, 1024])
    psT = [p.ps(f"psT{i}", [128, 1024], BF16) for i in range(2)]

    tload = [
        p.dma("sync", mu_sb[:, :], mu_cols),
        p.dma("sync", xb_sb[:, :], xb_cols),
        p.dma("gpsimd", bwr[:, :], bw_rep),
        p.dma("gpsimd", bgr[:, :], bg_rep),
        p.dma("sync", ident[:, :], ident_in),
    ]
    t_wx1 = p.dma("gpsimd", wx1_bf[:, :].rearrange("p (T n) -> p T n", T=16), Wx1.rearrange("(T p) n -> p T n", p=128))
    t_wx2 = p.dma("gpsimd", wx2_bf[:, :].rearrange("r (n h) -> r n h", n=5), Wx2n.rearrange("n r h -> r n h"))
    t_wwa = p.dma("gpsimd", wwa_bf[:, :].rearrange("p (T n) -> p T n", T=16), Wwa.rearrange("(T p) n -> p T n", p=128))
    t_wwb = p.dma("gpsimd", wwb_bf[:, :], Wwb)
    t_wga = p.dma("gpsimd", wga_bf[:, :].rearrange("p (T n) -> p T n", T=16), Wga.rearrange("(T p) n -> p T n", p=128))
    t_wgb = p.dma("gpsimd", wgb_bf[:, :], Wgb)

    scratch = {"sq": sq, "red": red}
    tr_toks = []
    xn_exports = []
    prev_ln = None
    ps_evac = {}
    for t in range(9):
        nrow = 128 if t < 8 else 1
        rows = slice(t * 128, t * 128 + nrow)
        tx = p.dma("sync", x_tile[0:nrow, :], x_ext[rows, :], after=[prev_ln])
        ln = emit_ln(p, x_tile[0:nrow, :], nrow, big_a[0:nrow, t * DM:(t + 1) * DM], scratch, [tx])
        prev_ln = ln
        xn_exports.append(p.dma("sync", xn_out[rows, :], big_a[0:nrow, t * DM:(t + 1) * DM], after=[ln]))
        for j in range(16):
            pt = psT[(t * 16 + j) % 2]
            war = ps_evac.get((t * 16 + j) - 2)
            if t < 8:
                tt = p.op("tensor", lambda te, t=t, j=j, pt=pt: te.transpose(
                    pt[:, 0:128], big_a[:, t * DM + j * 128: t * DM + j * 128 + 128], ident[:, :]),
                    after=[ln, tload[4], war])
                cp = p.op("vector", lambda v, t=t, j=j, pt=pt: v.tensor_copy(
                    xnT[j][:, t * 128: t * 128 + 128], pt[:, 0:128]), after=[tt])
            else:
                tt = p.op("tensor", lambda te, j=j, pt=pt: te.transpose(
                    pt[:, 0:1], big_a[0:1, 8 * DM + j * 128: 8 * DM + j * 128 + 128], ident[0:1, 0:1]),
                    after=[ln, tload[4], war])
                cp = p.op("vector", lambda v, j=j, pt=pt: v.tensor_copy(
                    xnT[j][:, 1024:1025], pt[:, 0:1]), after=[tt])
            ps_evac[t * 16 + j] = cp
            tr_toks.append(cp)

    lm_toks = []
    for j in range(16):
        prev = lm_toks[-1] if lm_toks else None
        td = p.op("vector", lambda v, j=j: v.tensor_sub(
            dlt[:, :], xnT[j][:, 0:1024], xnT[j][:, 1:1025]), after=tr_toks + xn_exports + [prev])
        tj = p.op("vector", lambda v, j=j: v.scalar_tensor_tensor(
            big_a[:, j * 1024:(j + 1) * 1024], dlt[:, :], mu_sb[:, j:j + 1],
            xnT[j][:, 1:1025], ALU.mult, ALU.add), after=[td, tload[0]])
        lm_toks.append(tj)

    lr_toks = [None] * 5
    msz = [128, 128, 64]
    prev_ev = None
    for mi in range(3):
        deps = (lm_toks + [t_wx1]) if mi == 0 else [prev_ev]
        last = None
        for k in range(16):
            for nj in range(2):
                last = p.op("tensor", lambda t, k=k, mi=mi, nj=nj: t.matmul(
                    psA[0:msz[mi], nj * 512:(nj + 1) * 512],
                    wx1_bf[:, k * 320 + mi * 128: k * 320 + mi * 128 + msz[mi]],
                    big_a[:, k * 1024 + nj * 512: k * 1024 + nj * 512 + 512],
                    start=(k == 0), stop=(k == 15)), after=deps)
                deps = []
        for h in range(2):
            n = 2 * mi + h
            if n >= 5:
                break
            tk = p.op("scalar", lambda s, n=n, h=h: s.activation(
                lr_n[n][:, :], psA[h * 64: h * 64 + 64, 0:1024], AF.Tanh), after=[last])
            lr_toks[n] = tk
            prev_ev = tk

    def lr_slice(n):
        return lr_n[n][:, :]

    psm_war = [None]

    def m_gen(n, war):
        lp_toks = []
        for j in range(16):
            d1 = ([lr_toks[n], t_wx2, tload[1]] if j == 0 else []) + [psm_war[0]]
            mm0 = p.op("tensor", lambda t, n=n, j=j: t.matmul(
                psM[:, 0:512], wx2_bf[:, n * DM + j * 128: n * DM + j * 128 + 128],
                lr_slice(n)[:, 0:512], start=True, stop=True), after=d1)
            mm1 = p.op("tensor", lambda t, n=n, j=j: t.matmul(
                psM[:, 512:1024], wx2_bf[:, n * DM + j * 128: n * DM + j * 128 + 128],
                lr_slice(n)[:, 512:1024], start=True, stop=True), after=[])
            prev_l2 = lp_toks[-1] if lp_toks else None
            eb = p.op("scalar", lambda s, n=n, j=j: s.activation(
                mt[:, :], psM[:, 0:1024], AF.Identity, bias=xb_sb[:, j * 5 + n: j * 5 + n + 1]),
                after=[mm0, mm1, prev_l2])
            psm_war[0] = eb
            ld_ = p.op("vector", lambda v, j=j: v.tensor_sub(
                dlt[:, :], xnT[j][:, 0:1024], xnT[j][:, 1:1025]), after=[eb])
            l1 = p.op("vector", lambda v, j=j: v.tensor_mul(
                mt[:, :], dlt[:, :], mt[:, :]), after=[eb, ld_])
            l2 = p.op("vector", lambda v, j=j: v.tensor_add(
                big_a[:, j * 1024:(j + 1) * 1024], xnT[j][:, 1:1025], mt[:, :]), after=[l1] + war)
            lp_toks.append(l2)
        return lp_toks

    def proj_plain(n, war, wb, wtok, out_dram):
        lp = m_gen(n, war)
        ev_last = {}

        def evac(mi, psd, last):
            b = ev[mi % 2]
            e1 = p.op("vector", lambda v, b=b, psd=psd: v.tensor_copy(b[:, :], psd[:, 0:1024]),
                      after=[last, ev_last.get(mi - 2)])
            st = p.dma("sync", out_dram[mi * 128:(mi + 1) * 128, :], b[:, :], after=[e1])
            ev_last[mi] = st
            return [e1, st]

        toks = emit_gemm(
            p,
            lambda k, mi: big_a[:, k * 1024 + mi * 128: k * 1024 + mi * 128 + 128],
            lambda k, nj, wb=wb: wb[:, k * 1024 + nj * 512: k * 1024 + nj * 512 + 512],
            [psA, psB], 16, 8, 2, 512, evac, lp + [wtok])
        return toks

    t_wr = p.dma("gpsimd", wbuf0[:, :].rearrange("p (T n) -> p T n", T=16), Wr.rearrange("(T p) n -> p T n", p=128))
    r_toks = proj_plain(0, list(lr_toks), wbuf0, t_wr, r_out)
    t_wk = p.dma("gpsimd", wbuf0[:, :].rearrange("p (T n) -> p T n", T=16), Wk.rearrange("(T p) n -> p T n", p=128), after=r_toks)
    k_toks = proj_plain(2, r_toks, wbuf0, t_wk, k_out)
    t_wv = p.dma("gpsimd", wbuf0[:, :].rearrange("p (T n) -> p T n", T=16), Wv.rearrange("(T p) n -> p T n", p=128), after=k_toks)
    v_toks = proj_plain(3, k_toks, wbuf0, t_wv, v_out)

    def proj_gated(n, war, wa, wb_, out_dram, brep, do_exp):
        lp = m_gen(n, war)
        last = None
        for k in range(16):
            for nj in range(2):
                aft = (lp + [t_wwa if n == 1 else t_wga, psm_war[0]]) if (k == 0 and nj == 0) else []
                last = p.op("tensor", lambda t, k=k, nj=nj, wa=wa: t.matmul(
                    psM[0:64, nj * 512:(nj + 1) * 512], wa[:, k * 64:(k + 1) * 64],
                    big_a[:, k * 1024 + nj * 512: k * 1024 + nj * 512 + 512],
                    start=(k == 0), stop=(k == 15)), after=aft)
        tzk = p.op("scalar", lambda s: s.activation(tz[:, :], psM[0:64, 0:1024], AF.Tanh), after=[last])
        psm_war[0] = tzk
        ev_last = {}

        def evac(mi, psd, last):
            b = ev[mi % 2]
            e1 = p.op("vector", lambda v, b=b, psd=psd: v.tensor_tensor(b[:, :], psd[:, 0:1024], brep[:, :], ALU.add),
                      after=[last, ev_last.get(mi - 2)])
            if do_exp:
                e2 = p.op("scalar", lambda s, b=b, mi=mi: s.activation(ev2[:, (mi % 2) * 1024:(mi % 2) * 1024 + 1024], b[:, :], AF.Exp), after=[e1, ev_last.get(mi - 2)])
                src_sb = ev2[:, (mi % 2) * 1024:(mi % 2) * 1024 + 1024]
            else:
                e2 = e1
                src_sb = b[:, :]
            st = p.dma("sync", out_dram[mi * 128:(mi + 1) * 128, :], src_sb, after=[e2])
            ev_last[mi] = st
            return [e2, st]

        toks = emit_gemm(
            p,
            lambda k, mi: tz[:, mi * 128:(mi + 1) * 128],
            lambda k, nj, wb_=wb_: wb_[:, nj * 512:(nj + 1) * 512],
            [psA, psB], 1, 8, 2, 512, evac, [tzk, t_wwb if n == 1 else t_wgb])
        return toks

    w_toks = proj_gated(1, v_toks, wwa_bf, wwb_bf, ew_out, bwr, True)
    g_toks = proj_gated(4, w_toks, wga_bf, wgb_bf, g_out, bgr, False)

    p.op("sync", lambda s: s.nop(), after=p.barrier_tokens())
    return p.realize()


def run_L1(inputs):
    nc = build_L1()
    x = inputs["L0_attn_x"]
    Wx2n = f32h(np.transpose(inputs["L0_attn_Wx2"], (1, 2, 0)))
    mu_cols = f32h(inputs["L0_attn_mu_x"].reshape(16, 128).T)
    xb_cols = f32h(np.transpose(inputs["L0_attn_x_bias"].reshape(5, 16, 128), (2, 1, 0)).reshape(128, 80))
    bw_rep = f32h(np.broadcast_to(inputs["L0_attn_bw"], (128, 1024)))
    bg_rep = f32h(np.broadcast_to(inputs["L0_attn_bg"], (128, 1024)))
    ident = np.eye(128, dtype=BF)
    shared = dict(
        Wx1=f32h(inputs["L0_attn_Wx1"]), Wx2n=Wx2n,
        Wr=f32h(inputs["L0_attn_Wr"]), Wk=f32h(inputs["L0_attn_Wk"]), Wv=f32h(inputs["L0_attn_Wv"]),
        Wwa=f32h(inputs["L0_attn_Ww_a"]), Wwb=f32h(inputs["L0_attn_Ww_b"]),
        Wga=f32h(inputs["L0_attn_Wg_a"]), Wgb=f32h(inputs["L0_attn_Wg_b"]),
        mu_cols=mu_cols, xb_cols=xb_cols, bw_rep=bw_rep, bg_rep=bg_rep, ident=ident,
    )
    in_maps = []
    for c in range(NCORE):
        b, hf = c // 2, c % 2
        xe = np.zeros((1025, DM), dtype=np.float32)
        xe[1:] = x[b, hf * 1024:(hf + 1) * 1024]
        if hf == 1:
            xe[0] = x[b, 1023]
        m = dict(shared)
        m["x_ext"] = xe
        in_maps.append(m)
    res = run_bass_kernel_spmd(nc, in_maps, core_ids=list(range(NCORE)))
    return res.results


def build_L2(T=2048, NCH=8, nchk=None):
    NT = T // 128
    NCHK = nchk if nchk is not None else T // C
    W = NCH * 64
    p = Prog()
    r_tok = p.din("r_tok", [T, W], F32)
    k_tok = p.din("k_tok", [T, W], F32)
    v_tok = p.din("v_tok", [T, W], F32)
    ew_tok = p.din("ew_tok", [T, W], F32)
    r_feat = p.din("r_feat", [W, T], F32)
    k_feat = p.din("k_feat", [W, T], F32)
    bonus_bc = p.din("bonus_bc", [64, W], F32)
    cfac = p.din("cfac", [64, 132], F32)
    c1t = p.din("c1t", [64, 64], F32)
    masku = p.din("masku", [64, W], F32)
    diagm = p.din("diagm", [64, W], F32)

    o_out = p.dout("o", [T, W], F32)

    rt = p.sb("rt", [128, NT * W], BF16)
    kt = p.sb("kt", [128, NT * W], BF16)
    vt = p.sb("vt", [128, NT * W], BF16)
    ew = p.sb("ew", [128, NT * W], F32)
    rf = [p.sb(f"rf{j}", [64, T], BF16) for j in range(NCH)]
    kf = [p.sb(f"kf{j}", [64, T], BF16) for j in range(NCH)]
    bon = p.sb("bon", [64, W], F32)
    cfac_sb = p.sb("cfac_sb", [64, 132], F32)
    c1t_sb = p.sb("c1t_sb", [64, 64], F32)
    masku_sb = p.sb("masku_sb", [64, W], F32)
    diagm_sb = p.sb("diagm_sb", [64, W], F32)
    pc_rt = p.sb("pc_rt", [64, W], BF16)
    pc_kt = p.sb("pc_kt", [64, W], BF16)
    pc_vt = p.sb("pc_vt", [64, W], BF16)
    pc_ew = p.sb("pc_ew", [64, W], F32)
    fac = p.sb("fac", [64, 8 * 256], F32)
    e1t = p.sb("e1t", [64, W], F32)
    ktn = p.sb("ktn", [64, W], BF16)
    qfb = p.sb("qfb", [64, W], BF16)
    kfb = p.sb("kfb", [64, W], BF16)
    dv = p.sb("dv", [64, W], F32)
    dvr = p.sb("dvr", [64, 8], F32)
    S = p.sb("S", [64, W], F32)
    Sf = p.sb("Sf", [64, W], F32)
    Sbf = p.sb("Sbf", [64, W], BF16)
    St = p.sb("St", [64, W], F32)
    asb = p.sb("asb", [64, W], F32)
    dd = p.sb("dd", [64, W], F32)
    absf = p.sb("absf", [64, W], BF16)
    osb = [p.sb(f"osb{i}", [64, W], F32) for i in range(2)]

    ps_fac = p.ps("ps_fac", [64, 2048])
    ps_e1 = p.ps("ps_e1", [64, 512])
    ps_A = p.ps("ps_A", [64, 512])
    ps_o = p.ps("ps_o", [64, 512])
    ps_d = p.ps("ps_d", [64, 512])

    tL = [
        p.dma("gpsimd", rt[:, :].rearrange("p (T n) -> p T n", T=NT), r_tok.rearrange("(T p) n -> p T n", p=128)),
        p.dma("gpsimd", kt[:, :].rearrange("p (T n) -> p T n", T=NT), k_tok.rearrange("(T p) n -> p T n", p=128)),
        p.dma("gpsimd", vt[:, :].rearrange("p (T n) -> p T n", T=NT), v_tok.rearrange("(T p) n -> p T n", p=128)),
        p.dma("sync", ew[:, :].rearrange("p (T n) -> p T n", T=NT), ew_tok.rearrange("(T p) n -> p T n", p=128)),
        p.dma("sync", bon[:, :], bonus_bc),
        p.dma("sync", cfac_sb[:, :], cfac),
        p.dma("sync", c1t_sb[:, :], c1t),
        p.dma("sync", masku_sb[:, :], masku),
        p.dma("sync", diagm_sb[:, :], diagm),
    ]
    tF = []
    for j in range(NCH):
        tF.append(p.dma("gpsimd", rf[j][:, :], r_feat[j * 64:(j + 1) * 64, :]))
        tF.append(p.dma("gpsimd", kf[j][:, :], k_feat[j * 64:(j + 1) * 64, :]))
    z0 = p.op("vector", lambda v: v.memset(S[:, :], 0.0))

    Em8 = fac[:, 128:2048:256].unsqueeze(2).broadcast_to([64, 8, 64])
    Ec8 = fac[:, 129:2048:256].unsqueeze(2).broadcast_to([64, 8, 64])
    dvr_rep = dvr[:, :].unsqueeze(2).broadcast_to([64, 8, 64])

    st_prev = {}
    last = {"upd": z0, "oev": None, "facread": None, "e1read": None, "ktnread": None,
            "qkread": None, "aev": None, "dvread": None, "sbfread": None, "stincr": None,
            "ddread": None, "pcread": [None] * 4}
    for c in range(NCHK):
        tt_i = (c * C) // 128
        rb = (c * C) % 128
        pr = last["pcread"]
        sg = [
            p.dma("sync", pc_rt[:, :], rt[rb:rb + C, tt_i * W:(tt_i + 1) * W], after=[tL[0], pr[0]]),
            p.dma("sync", pc_kt[:, :], kt[rb:rb + C, tt_i * W:(tt_i + 1) * W], after=[tL[1], pr[1]]),
            p.dma("sync", pc_vt[:, :], vt[rb:rb + C, tt_i * W:(tt_i + 1) * W], after=[tL[2], pr[2]]),
            p.dma("sync", pc_ew[:, :], ew[rb:rb + C, tt_i * W:(tt_i + 1) * W], after=[tL[3], pr[3]]),
        ]
        fm = []
        for j in range(NCH):
            fm.append(p.op("tensor", lambda t, j=j: t.matmul(
                ps_fac[:, j * 256: j * 256 + 132], pc_ew[:, j * 64:(j + 1) * 64],
                cfac_sb[:, :], start=True, stop=True),
                after=([sg[3], tL[5], last["facread"]] if j == 0 else [])))
        fx = p.op("scalar", lambda s: s.activation(fac[:, :], ps_fac[:, :], AF.Exp), after=fm)
        e1m = p.op("tensor", lambda t: t.matmul(
            ps_e1[:, 0:W], c1t_sb[:, :], pc_ew[:, :], start=True, stop=True),
            after=[tL[6], sg[3], last["e1read"]])
        e1x = p.op("scalar", lambda s: s.activation(e1t[:, :], ps_e1[:, 0:W], AF.Exp), after=[e1m])
        ktn_t = p.op("vector", lambda v: v.tensor_mul(ktn[:, :], pc_kt[:, :], e1t[:, :]),
                     after=[e1x, sg[1], last["ktnread"]])
        qk_t = []
        for j in range(NCH):
            qt = p.op("vector", lambda v, j=j, c=c: v.tensor_mul(
                qfb[:, j * 64:(j + 1) * 64], rf[j][:, c * C:(c + 1) * C],
                fac[:, j * 256 + 64: j * 256 + 128]), after=[fx, tF[2 * j], last["qkread"]])
            ktf = p.op("vector", lambda v, j=j, c=c: v.tensor_mul(
                kfb[:, j * 64:(j + 1) * 64], kf[j][:, c * C:(c + 1) * C],
                fac[:, j * 256: j * 256 + 64]), after=[fx, tF[2 * j + 1], last["qkread"]])
            qk_t += [qt, ktf]
        d1 = p.op("vector", lambda v: v.tensor_mul(dv[:, :], pc_rt[:, :], pc_kt[:, :]),
                  after=[sg[0], sg[1], last["dvread"]])
        d2 = p.op("vector", lambda v: v.tensor_mul(dv[:, :], dv[:, :], bon[:, :]), after=[d1, tL[4]])
        d3 = p.op("vector", lambda v: v.tensor_reduce(
            dvr[:, :], dv[:, :].rearrange("p (g x) -> p g x", g=8), AX.X, ALU.add), after=[d2])
        s1 = p.op("vector", lambda v: v.tensor_tensor(
            Sf[:, :].rearrange("p (g x) -> p g x", g=8), S[:, :].rearrange("p (g x) -> p g x", g=8),
            Em8, ALU.mult), after=[fx, last["upd"]])
        s2 = p.op("vector", lambda v: v.tensor_copy(Sbf[:, :], Sf[:, :]), after=[s1, last["sbfread"]])
        am = []
        for j in range(NCH):
            am.append(p.op("tensor", lambda t, j=j: t.matmul(
                ps_A[:, j * 64:(j + 1) * 64], kfb[:, j * 64:(j + 1) * 64],
                qfb[:, j * 64:(j + 1) * 64], start=True, stop=True),
                after=(qk_t + [last["aev"]] if j == 0 else [])))
        a1 = p.op("vector", lambda v: v.tensor_scalar(asb[:, :], ps_A[:, 0:W], 1e30, None, ALU.min), after=am)
        a2 = p.op("vector", lambda v: v.tensor_mul(asb[:, :], asb[:, :], masku_sb[:, :]), after=[a1, tL[7]])
        a3 = p.op("vector", lambda v, dvr_rep=dvr_rep: v.tensor_tensor(
            dd[:, :].rearrange("p (g x) -> p g x", g=8), diagm_sb[:, :].rearrange("p (g x) -> p g x", g=8),
            dvr_rep, ALU.mult), after=[d3, tL[8], last["ddread"]])
        a4 = p.op("vector", lambda v: v.tensor_add(absf[:, :], asb[:, :], dd[:, :]), after=[a2, a3])
        om = []
        for j in range(NCH):
            om.append(p.op("tensor", lambda t, j=j: t.matmul(
                ps_o[:, j * 64:(j + 1) * 64], absf[:, j * 64:(j + 1) * 64],
                pc_vt[:, j * 64:(j + 1) * 64], start=True, stop=False),
                after=([a4, sg[2], last["oev"]] if j == 0 else [])))
            om.append(p.op("tensor", lambda t, j=j: t.matmul(
                ps_o[:, j * 64:(j + 1) * 64], qfb[:, j * 64:(j + 1) * 64],
                Sbf[:, j * 64:(j + 1) * 64], start=False, stop=True),
                after=[s2] if j == 0 else []))
        ob_ = osb[c % 2]
        oev = p.op("vector", lambda v, ob_=ob_: v.tensor_copy(ob_[:, :], ps_o[:, 0:W]),
                   after=om + [st_prev.get(c - 2)])
        st = p.dma("sync", o_out[c * C:(c + 1) * C, :], ob_[:, :], after=[oev])
        st_prev[c] = st
        dm = []
        for j in range(NCH):
            dm.append(p.op("tensor", lambda t, j=j: t.matmul(
                ps_d[:, j * 64:(j + 1) * 64], ktn[:, j * 64:(j + 1) * 64],
                pc_vt[:, j * 64:(j + 1) * 64], start=True, stop=True),
                after=([ktn_t, last["stincr"]] if j == 0 else [])))
        u1 = p.op("vector", lambda v: v.tensor_add(St[:, :], Sf[:, :], ps_d[:, 0:W]), after=dm + [s1])
        u2 = p.op("vector", lambda v: v.tensor_tensor(
            S[:, :].rearrange("p (g x) -> p g x", g=8), St[:, :].rearrange("p (g x) -> p g x", g=8),
            Ec8, ALU.mult), after=[u1])
        last = {"upd": u2, "oev": oev, "facread": u2, "e1read": ktn_t,
                "ktnread": dm[-1], "qkread": om[-1], "aev": om[-1], "dvread": a3,
                "sbfread": om[-1], "stincr": u1, "ddread": a4,
                "pcread": [d1, ktn_t, dm[-1], e1m]}
    p.op("sync", lambda s: s.nop(), after=p.barrier_tokens())
    return p.realize()


def l2_consts():
    mid = 31
    C1 = np.zeros((C, C), dtype=np.float32)
    C2 = np.zeros((C, C), dtype=np.float32)
    for s in range(C):
        for sp in range(C):
            if s < sp <= mid:
                C1[sp, s] = -1.0
            elif mid < sp <= s:
                C1[sp, s] = 1.0
            if s <= sp <= mid:
                C2[sp, s] = 1.0
            elif mid < sp <= s - 1:
                C2[sp, s] = -1.0
    midc = np.zeros((C, 1), dtype=np.float32)
    midc[:mid + 1, 0] = -1.0
    endc = np.zeros((C, 1), dtype=np.float32)
    endc[mid + 1:, 0] = -1.0
    cfac = np.concatenate([C1, C2, midc, endc, np.zeros((C, 2), np.float32)], axis=1)
    masku = np.tile(np.triu(np.ones((C, C), np.float32), 1), (1, 8))
    diagm = np.tile(np.eye(C, dtype=np.float32), (1, 8))
    return f32h(cfac), f32h(C1), f32h(masku), f32h(diagm)


def run_L2(l1res, inputs, nchk=None):
    nc = build_L2(nchk=nchk)
    cfac, c1t, masku, diagm = l2_consts()
    bonus = inputs["L0_attn_bonus"]
    full = {}
    for nm in ("r", "k", "v", "expw"):
        a = np.zeros((4, 2048, 1024), dtype=np.float32)
        for c in range(NCORE):
            b, hf = c // 2, c % 2
            a[b, hf * 1024:(hf + 1) * 1024] = np.asarray(l1res[c][nm])
        full[nm] = a.reshape(4, 2048, 16, 64)
    in_maps = []
    for c2 in range(NCORE):
        b, h0 = c2 // 2, (c2 % 2) * 8
        sel = lambda nm: np.ascontiguousarray(full[nm][b, :, h0:h0 + 8].reshape(2048, 512))
        selT = lambda nm: np.ascontiguousarray(full[nm][b, :, h0:h0 + 8].reshape(2048, 512).T)
        in_maps.append(dict(
            r_tok=sel("r"), k_tok=sel("k"), v_tok=sel("v"), ew_tok=sel("expw"),
            r_feat=selT("r"), k_feat=selT("k"),
            bonus_bc=f32h(np.broadcast_to(bonus[h0:h0 + 8].reshape(1, 512), (64, 512))),
            cfac=cfac, c1t=c1t, masku=masku, diagm=diagm,
        ))
    res = run_bass_kernel_spmd(nc, in_maps, core_ids=list(range(NCORE)))
    o = np.zeros((4, 2048, 16, 64), dtype=np.float32)
    for c2 in range(NCORE):
        b, h0 = c2 // 2, (c2 % 2) * 8
        o[b, :, h0:h0 + 8] = np.asarray(res.results[c2]["o"]).reshape(2048, 8, 64)
    return o


def build_L3():
    p = Prog()
    o_in = p.din("o_in", [TOK, 1024], F32)
    g_in = p.din("g_in", [TOK, 1024], F32)
    lnw_rep = p.din("lnw_rep", [128, 1024], F32)
    lnb_rep = p.din("lnb_rep", [128, 1024], F32)
    Wo = p.din("Wo", [1024, DM], F32)
    ident_in = p.din("ident", [128, 128], BF16)
    s_out = p.dout("s", [TOK, DM], F32)

    ot = [p.sb(f"ot{i}", [128, 1024], F32) for i in range(2)]
    gt = [p.sb(f"gt{i}", [128, 1024], F32) for i in range(2)]
    lnw = p.sb("lnw", [128, 1024], F32)
    lnb = p.sb("lnb", [128, 1024], F32)
    ident = p.sb("ident_sb", [128, 128], BF16)
    wo_bf = p.sb("wo_bf", [128, 8 * DM], BF16)
    sq = p.sb("sq3", [128, 1024], F32)
    red = p.sb("red3", [128, 8 * 16], F32)
    on = p.sb("on", [128, 1024], F32)
    og = p.sb("og", [128, 1024], BF16)
    ogT = p.sb("ogT", [128, 8 * 1024], BF16)
    ssb = [p.sb(f"ssb{i}", [128, DM], F32) for i in range(2)]

    psT = [p.ps(f"psT{i}", [128, 1024], BF16) for i in range(2)]
    psG = p.ps("psG", [128, DM])

    t_cons = [
        p.dma("sync", lnw[:, :], lnw_rep),
        p.dma("sync", lnb[:, :], lnb_rep),
        p.dma("sync", ident[:, :], ident_in),
    ]
    t_wo = p.dma("gpsimd", wo_bf[:, :].rearrange("p (T n) -> p T n", T=8), Wo.rearrange("(T p) n -> p T n", p=128))

    tr_done = []
    prev = {}
    for ti in range(8):
        ob_, gb_ = ot[ti % 2], gt[ti % 2]
        tl1 = p.dma("sync", ob_[:, :], o_in[ti * 128:(ti + 1) * 128, :], after=[prev.get(("o", ti - 2))])
        tl2 = p.dma("sync", gb_[:, :], g_in[ti * 128:(ti + 1) * 128, :], after=[prev.get(("g", ti - 2))])
        rr = lambda i: red[:, ti % 2 * 64 + i * 16: ti % 2 * 64 + i * 16 + 16]
        ov = ob_[:, :].rearrange("p (g x) -> p g x", g=16)
        m1 = p.op("vector", lambda v, ov=ov, rr=rr: v.tensor_reduce(rr(0), ov, AX.X, ALU.add), after=[tl1, prev.get(("rd", ti - 2))])
        m1b = p.op("vector", lambda v, rr=rr: v.tensor_scalar(rr(0), rr(0), 1.0 / 64, None, ALU.mult), after=[m1])
        q1 = p.op("scalar", lambda s, ob_=ob_: s.activation(sq[:, :], ob_[:, :], AF.Square), after=[tl1, prev.get(("sq", ti - 1))])
        m2 = p.op("vector", lambda v, rr=rr: v.tensor_reduce(rr(1), sq[:, :].rearrange("p (g x) -> p g x", g=16), AX.X, ALU.add), after=[q1])
        m2b = p.op("vector", lambda v, rr=rr: v.tensor_scalar(rr(1), rr(1), 1.0 / 64, None, ALU.mult), after=[m2])
        m3 = p.op("vector", lambda v, rr=rr: v.tensor_tensor(rr(2), rr(0), rr(0), ALU.mult), after=[m1b])
        m4 = p.op("vector", lambda v, rr=rr: v.tensor_tensor(rr(2), rr(1), rr(2), ALU.subtract), after=[m2b, m3])
        m5 = p.op("vector", lambda v, rr=rr: v.tensor_scalar(rr(2), rr(2), 1e-5, None, ALU.add), after=[m4])
        m6 = p.op("scalar", lambda s, rr=rr: s.activation(rr(3), rr(2), AF.Sqrt), after=[m5])
        m7 = p.op("vector", lambda v, rr=rr: v.reciprocal(rr(3), rr(3)), after=[m6])
        mrep = lambda i: red[:, ti % 2 * 64 + i * 16: ti % 2 * 64 + i * 16 + 16].unsqueeze(2).broadcast_to([128, 16, 64])
        n1 = p.op("vector", lambda v, ov=ov, mrep=mrep: v.tensor_tensor(
            on[:, :].rearrange("p (g x) -> p g x", g=16), ov, mrep(0), ALU.subtract), after=[m1b, tl1, prev.get(("on", ti - 1))])
        n2 = p.op("vector", lambda v, mrep=mrep: v.tensor_tensor(
            on[:, :].rearrange("p (g x) -> p g x", g=16), on[:, :].rearrange("p (g x) -> p g x", g=16), mrep(3), ALU.mult), after=[n1, m7])
        n3 = p.op("vector", lambda v: v.tensor_mul(on[:, :], on[:, :], lnw[:, :]), after=[n2, t_cons[0]])
        n4 = p.op("vector", lambda v: v.tensor_add(on[:, :], on[:, :], lnb[:, :]), after=[n3, t_cons[1]])
        sgm = p.op("scalar", lambda s, gb_=gb_: s.activation(sq[:, :], gb_[:, :], AF.Sigmoid), after=[tl2, m2])
        gg = p.op("vector", lambda v, gb_=gb_: v.tensor_mul(sq[:, :], sq[:, :], gb_[:, :]), after=[sgm])
        ogk = p.op("vector", lambda v: v.tensor_mul(og[:, :], on[:, :], sq[:, :]), after=[gg, n4, prev.get(("og", ti - 1))])
        prev[("sq", ti)] = ogk
        prev[("rd", ti)] = n2
        prev[("on", ti)] = ogk
        cps = []
        for j in range(8):
            pt = psT[(ti * 8 + j) % 2]
            war = prev.get(("tp", ti * 8 + j - 2))
            tt = p.op("tensor", lambda te, j=j, pt=pt: te.transpose(
                pt[:, 0:128], og[:, j * 128:(j + 1) * 128], ident[:, :]), after=[ogk, t_cons[2], war])
            cp = p.op("vector", lambda v, j=j, ti=ti, pt=pt: v.tensor_copy(
                ogT[:, j * 1024 + ti * 128: j * 1024 + ti * 128 + 128], pt[:, 0:128]), after=[tt])
            prev[("tp", ti * 8 + j)] = cp
            cps.append(cp)
        prev[("og", ti)] = cps[-1]
        prev[("o", ti)] = cps[-1]
        prev[("g", ti)] = ogk
        tr_done += cps

    ev_last = {}
    for mi in range(8):
        aft = tr_done + [t_wo] if mi == 0 else [ev_last[mi - 1]]
        lastmm = None
        for k in range(8):
            for nj in range(4):
                lastmm = p.op("tensor", lambda t, k=k, mi=mi, nj=nj: t.matmul(
                    psG[:, nj * 512:(nj + 1) * 512],
                    ogT[:, k * 1024 + mi * 128: k * 1024 + mi * 128 + 128],
                    wo_bf[:, k * DM + nj * 512: k * DM + nj * 512 + 512],
                    start=(k == 0), stop=(k == 7)), after=aft)
                aft = []
        b = ssb[mi % 2]
        e1 = p.op("vector", lambda v, b=b: v.tensor_copy(b[:, :], psG[:, :]), after=[lastmm, ev_last.get(mi - 2)])
        st = p.dma("sync", s_out[mi * 128:(mi + 1) * 128, :], b[:, :], after=[e1])
        ev_last[mi] = st
    p.op("sync", lambda s: s.nop(), after=p.barrier_tokens())
    return p.realize()


def run_L3(o_full, l1res, inputs):
    nc = build_L3()
    ln_w = inputs["L0_attn_ln_w"].astype(np.float32)
    ln_b = inputs["L0_attn_ln_b"].astype(np.float32)
    lnw_rep = f32h(np.broadcast_to(np.tile(ln_w, 16), (128, 1024)))
    lnb_rep = f32h(np.broadcast_to(np.tile(ln_b, 16), (128, 1024)))
    ident = np.eye(128, dtype=BF)
    in_maps = []
    for c in range(NCORE):
        b, hf = c // 2, c % 2
        in_maps.append(dict(
            o_in=f32h(o_full[b, hf * 1024:(hf + 1) * 1024].reshape(1024, 1024)),
            g_in=f32h(np.asarray(l1res[c]["g"])),
            lnw_rep=lnw_rep, lnb_rep=lnb_rep,
            Wo=f32h(inputs["L0_attn_Wo"]), ident=ident,
        ))
    res = run_bass_kernel_spmd(nc, in_maps, core_ids=list(range(NCORE)))
    return [np.asarray(res.results[c]["s"]) for c in range(NCORE)]


def build_L4():
    IB = 512
    NB = 5632 // IB
    p = Prog(dma_scratch=1024)
    xnT_in = p.din("xnT", [DM, TOK], BF16)
    WguT = p.din("WguT", [DM, 11264], BF16)
    Wd0T = p.din("Wd0T", [5632, DM], BF16)
    x_in = p.din("x_in", [TOK, DM], F32)
    s_in = p.din("s_in", [TOK, DM], F32)
    x1_out = p.dout("x1", [TOK, DM], F32)

    xnT = p.sb("xnT_sb", [128, 16 * 1024], BF16)
    yac = p.sb("yac", [128, 8 * DM], F32)
    hb = p.sb("hb", [128, 4 * 1024], BF16)
    wg = [p.sb(f"wg{i}", [128, 16 * 512], BF16) for i in range(2)]
    wu = [p.sb(f"wu{i}", [128, 16 * 512], BF16) for i in range(2)]
    wd = [p.sb(f"wd{i}", [128, 4 * DM], BF16) for i in range(2)]
    xtile = p.sb("xtile", [128, DM], F32)
    stile = p.sb("stile", [128, DM], F32)
    gsb_t = p.sb("gsb4", [128, 1024], F32)
    gsb = gsb_t[:, :]
    red = p.sb("red4", [128, 4], F32)

    psA = p.ps("psA4", [128, 1024])
    psB = p.ps("psB4", [128, 1024])
    psD = p.ps("psD4", [128, DM])

    t_xn = p.dma("sync", xnT[:, :].rearrange("p (T n) -> p T n", T=16), xnT_in.rearrange("(T p) n -> p T n", p=128))

    wtoks = {}
    for blk in range(2):
        wtoks[("g", blk)] = p.dma("scalar", wg[blk][:, :].rearrange("p (T n) -> p T n", T=16),
                                  WguT[:, blk * IB:(blk + 1) * IB].rearrange("(T p) n -> p T n", p=128))
        wtoks[("u", blk)] = p.dma("scalar", wu[blk][:, :].rearrange("p (T n) -> p T n", T=16),
                                  WguT[:, 5632 + blk * IB: 5632 + (blk + 1) * IB].rearrange("(T p) n -> p T n", p=128))
        wtoks[("d", blk)] = p.dma("scalar", wd[blk][:, :].rearrange("p (T n) -> p T n", T=4),
                                  Wd0T[blk * IB:(blk + 1) * IB, :].rearrange("(T p) n -> p T n", p=128))

    yev = {}
    hread = {}
    for blk in range(NB):
        bi = blk % 2
        wgb, wub, wdb = wg[bi], wu[bi], wd[bi]
        tg, tu, td = wtoks[("g", blk)], wtoks[("u", blk)], wtoks[("d", blk)]
        hlast = []
        prev_e1 = prev_e2 = None
        for mi in range(4):
            aft = [t_xn, tg, hread.get((blk - 1, mi), None), prev_e1]
            glast = None
            for k in range(16):
                for nj in range(2):
                    glast = p.op("tensor", lambda t, k=k, mi=mi, nj=nj, wgb=wgb: t.matmul(
                        psA[:, nj * 512:(nj + 1) * 512], wgb[:, k * 512 + mi * 128: k * 512 + mi * 128 + 128],
                        xnT[:, k * 1024 + nj * 512: k * 1024 + nj * 512 + 512],
                        start=(k == 0), stop=(k == 15)), after=aft)
                    aft = []
            aft = [tu, prev_e2]
            ulast = None
            for k in range(16):
                for nj in range(2):
                    ulast = p.op("tensor", lambda t, k=k, mi=mi, nj=nj, wub=wub: t.matmul(
                        psB[:, nj * 512:(nj + 1) * 512], wub[:, k * 512 + mi * 128: k * 512 + mi * 128 + 128],
                        xnT[:, k * 1024 + nj * 512: k * 1024 + nj * 512 + 512],
                        start=(k == 0), stop=(k == 15)), after=aft)
                    aft = []
            e1 = p.op("scalar", lambda s: s.activation(gsb, psA[:, :], AF.Silu), after=[glast, prev_e2])
            e2 = p.op("vector", lambda v, mi=mi: v.tensor_tensor(hb[:, mi * 1024:(mi + 1) * 1024], gsb, psB[:, :], ALU.mult), after=[ulast, e1])
            prev_e1, prev_e2 = e1, e2
            hlast.append(e2)
        dlast = None
        for ti in range(8):
            aft = hlast + [td, yev.get((blk, ti - 1)), yev.get((blk - 1, ti))]
            for k2 in range(4):
                for nj in range(4):
                    dlast = p.op("tensor", lambda t, k2=k2, ti=ti, nj=nj, wdb=wdb: t.matmul(
                        psD[:, nj * 512:(nj + 1) * 512], hb[:, k2 * 1024 + ti * 128: k2 * 1024 + ti * 128 + 128],
                        wdb[:, k2 * DM + nj * 512: k2 * DM + nj * 512 + 512],
                        start=(k2 == 0), stop=(k2 == 3)), after=aft)
                    aft = []
            if blk == 0:
                ev = p.op("vector", lambda v, ti=ti: v.tensor_copy(yac[:, ti * DM:(ti + 1) * DM], psD[:, :]), after=[dlast])
            else:
                ev = p.op("vector", lambda v, ti=ti: v.tensor_add(yac[:, ti * DM:(ti + 1) * DM], yac[:, ti * DM:(ti + 1) * DM], psD[:, :]), after=[dlast, yev.get((blk - 1, ti))])
            yev[(blk, ti)] = ev
        for mi in range(4):
            hread[(blk, mi)] = dlast
        if blk + 2 < NB:
            wtoks[("g", blk + 2)] = p.dma("scalar", wgb[:, :].rearrange("p (T n) -> p T n", T=16),
                                          WguT[:, (blk + 2) * IB:(blk + 3) * IB].rearrange("(T p) n -> p T n", p=128),
                                          after=[p for p in [hlast[-1]] if p])
            wtoks[("u", blk + 2)] = p.dma("scalar", wub[:, :].rearrange("p (T n) -> p T n", T=16),
                                          WguT[:, 5632 + (blk + 2) * IB: 5632 + (blk + 3) * IB].rearrange("(T p) n -> p T n", p=128),
                                          after=[hlast[-1]])
            wtoks[("d", blk + 2)] = p.dma("scalar", wdb[:, :].rearrange("p (T n) -> p T n", T=4),
                                          Wd0T[(blk + 2) * IB:(blk + 3) * IB, :].rearrange("(T p) n -> p T n", p=128),
                                          after=[yev[(blk, 7)]])

    stprev = {}
    for ti in range(8):
        yt = yac[:, ti * DM:(ti + 1) * DM]
        tx = p.dma("sync", xtile[:, :], x_in[ti * 128:(ti + 1) * 128, :], after=[stprev.get(ti - 1)])
        q1 = p.op("scalar", lambda s, yt=yt: s.activation(stile[:, :], yt, AF.Square, accum_out=red[:, 0:1]),
                  after=[yev[(NB - 1, ti)], stprev.get(ti - 1)])
        ts_ = p.dma("sync", stile[:, :], s_in[ti * 128:(ti + 1) * 128, :], after=[q1])
        q2 = p.op("vector", lambda v: v.tensor_scalar(red[:, 1:2], red[:, 0:1], 1.0 / DM, 1e-6, ALU.mult, ALU.add), after=[q1])
        q3 = p.op("scalar", lambda s: s.activation(red[:, 2:3], red[:, 1:2], AF.Sqrt), after=[q2])
        q4 = p.op("vector", lambda v: v.reciprocal(red[:, 3:4], red[:, 2:3]), after=[q3])
        q5 = p.op("vector", lambda v, yt=yt: v.tensor_scalar(yt, yt, red[:, 3:4], None, ALU.mult), after=[q4])
        q6 = p.op("scalar", lambda s, yt=yt: s.activation(yt, yt, AF.Silu), after=[q5])
        q7 = p.op("vector", lambda v, yt=yt: v.tensor_mul(stile[:, :], yt, stile[:, :]), after=[q6, ts_])
        q8 = p.op("vector", lambda v: v.tensor_add(stile[:, :], stile[:, :], xtile[:, :]), after=[q7, tx])
        st = p.dma("sync", x1_out[ti * 128:(ti + 1) * 128, :], stile[:, :], after=[q8])
        stprev[ti] = st
    p.op("sync", lambda s: s.nop(), after=p.barrier_tokens())
    return p.realize()


def run_L4(deq, l1res, s_list, inputs):
    nc = build_L4()
    x = inputs["L0_attn_x"]
    in_maps = []
    for c in range(NCORE):
        b, hf = c // 2, c % 2
        xn = np.asarray(l1res[c]["xn"])[1:]
        in_maps.append(dict(
            xnT=np.ascontiguousarray(xn.T),
            WguT=deq["W_guT"], Wd0T=deq["W_d0T"],
            x_in=f32h(x[b, hf * 1024:(hf + 1) * 1024]),
            s_in=f32h(s_list[c]),
        ))
    res = run_bass_kernel_spmd(nc, in_maps, core_ids=list(range(NCORE)))
    return [np.asarray(res.results[c]["x1"]) for c in range(NCORE)]


def build_L5():
    p = Prog()
    x1_in = p.din("x1_in", [TOK, DM], F32)
    WqT = p.din("WqT", [DM, DM], F32)
    WkT = p.din("WkT", [DM, 512], F32)
    WvT = p.din("WvT", [DM, 512], F32)
    bq_rep = p.din("bq_rep", [128, DM], F32)
    bk_rep = p.din("bk_rep", [128, 512], F32)
    bv_rep = p.din("bv_rep", [128, 512], F32)
    ident_in = p.din("ident", [128, 128], F16)
    ones_in = p.din("ones_col", [128, 1], F16)

    xn1_out = p.dout("xn1", [TOK, DM], F16)
    sq_out = p.dout("sq", [TOK, DM], F32)
    kvp_out = p.dout("kvp", [128, 516], F32)

    xn1b = p.sb("xn1b", [128, 8 * DM], F16)
    xn1T = p.sb("xn1T", [128, 16 * 1024], F16)
    wq_bf = p.sb("wq_bf", [128, 16 * DM], F16)
    wk_bf = p.sb("wk_bf", [128, 16 * 512], F16)
    wv_bf = p.sb("wv_bf", [128, 16 * 512], F16)
    bqr = p.sb("bqr", [128, DM], F32)
    bkr = p.sb("bkr", [128, 512], F32)
    bvr = p.sb("bvr", [128, 512], F32)
    ident = p.sb("ident5", [128, 128], F16)
    ones_c = p.sb("ones5", [128, 1], F16)
    x_tile = p.sb("x_tile5", [128, DM], F32)
    sq_s = p.sb("sq_s5", [128, DM], F32)
    red = p.sb("red5", [128, 4], F32)
    xf = xn1b[:, :].bitcast(F32)
    qsb = xf[:, 0:2048]
    qm = xf[:, 2048:4096]
    sk_bf = p.sb("sk_bf", [128, 8 * 512], F16)
    v_bf = p.sb("v_bf", [128, 8 * 512], F16)
    kvev = p.sb("kvev", [128, 516], F32)

    psQ = p.ps("psQ", [128, DM])
    psK2 = p.ps("psK2", [128, 1024])
    psK = psK2
    psKV = psK2
    psT = [p.ps(f"psT5{i}", [128, 1024], F16) for i in range(2)]

    tcons = [
        p.dma("sync", bqr[:, :], bq_rep),
        p.dma("sync", bkr[:, :], bk_rep),
        p.dma("sync", bvr[:, :], bv_rep),
        p.dma("sync", ident[:, :], ident_in),
        p.dma("sync", ones_c[:, :], ones_in),
    ]
    t_wq = p.dma("gpsimd", wq_bf[:, :].rearrange("p (T n) -> p T n", T=16), WqT.rearrange("(T p) n -> p T n", p=128))
    t_wk = p.dma("gpsimd", wk_bf[:, :].rearrange("p (T n) -> p T n", T=16), WkT.rearrange("(T p) n -> p T n", p=128))
    t_wv = p.dma("gpsimd", wv_bf[:, :].rearrange("p (T n) -> p T n", T=16), WvT.rearrange("(T p) n -> p T n", p=128))

    scratch = {"sq": sq_s, "red": red}
    tr_toks = []
    xn_exp = []
    prev_ln = None
    ps_evac = {}
    for t in range(8):
        tx = p.dma("sync", x_tile[:, :], x1_in[t * 128:(t + 1) * 128, :], after=[prev_ln])
        ln = emit_ln(p, x_tile[:, :], 128, xn1b[:, t * DM:(t + 1) * DM], scratch, [tx])
        prev_ln = ln
        xn_exp.append(p.dma("sync", xn1_out[t * 128:(t + 1) * 128, :], xn1b[:, t * DM:(t + 1) * DM], after=[ln]))
        for j in range(16):
            pt = psT[(t * 16 + j) % 2]
            war = ps_evac.get((t * 16 + j) - 2)
            tt = p.op("tensor", lambda te, t=t, j=j, pt=pt: te.transpose(
                pt[:, 0:128], xn1b[:, t * DM + j * 128: t * DM + j * 128 + 128], ident[:, :]),
                after=[ln, tcons[3], war])
            cp = p.op("vector", lambda v, t=t, j=j, pt=pt: v.tensor_copy(
                xn1T[:, j * 1024 + t * 128: j * 1024 + t * 128 + 128], pt[:, 0:128]), after=[tt])
            ps_evac[t * 16 + j] = cp
            tr_toks.append(cp)

    ev_last = {}
    overlay_ok = tr_toks + xn_exp
    for mi in range(8):
        aft = overlay_ok + [t_wq] if mi == 0 else [ev_last[mi - 1]]
        lastmm = None
        for k in range(16):
            for nj in range(4):
                lastmm = p.op("tensor", lambda t, k=k, mi=mi, nj=nj: t.matmul(
                    psQ[:, nj * 512:(nj + 1) * 512],
                    xn1T[:, k * 1024 + mi * 128: k * 1024 + mi * 128 + 128],
                    wq_bf[:, k * DM + nj * 512: k * DM + nj * 512 + 512],
                    start=(k == 0), stop=(k == 15)), after=aft)
                aft = []
        e1 = p.op("vector", lambda v: v.tensor_add(qsb, psQ[:, :], bqr[:, :]),
                  after=[lastmm, tcons[0], ev_last.get(mi - 1)] + (overlay_ok if mi == 0 else []))
        e2 = p.op("vector", lambda v: v.tensor_scalar(qm, qsb, 0.0, None, ALU.min), after=[e1])
        e3 = p.op("scalar", lambda s: s.activation(qm, qm, AF.Exp), after=[e2])
        e4 = p.op("vector", lambda v: v.tensor_scalar(qsb, qsb, 0.0, None, ALU.max), after=[e3])
        e5 = p.op("vector", lambda v: v.tensor_add(qsb, qsb, qm), after=[e4])
        st = p.dma("sync", sq_out[mi * 128:(mi + 1) * 128, :], qsb, after=[e5])
        ev_last[mi] = st

    def kv_proj(wbf, wtok, brep_sb, btok, do_elu, dst, first_extra):
        evl = {}
        for mi in range(8):
            aft = (tr_toks + [wtok] + first_extra) if mi == 0 else [evl[mi - 1]]
            lastmm = None
            for k in range(16):
                lastmm = p.op("tensor", lambda t, k=k, mi=mi, wbf=wbf: t.matmul(
                    psK[:, 0:512],
                    xn1T[:, k * 1024 + mi * 128: k * 1024 + mi * 128 + 128],
                    wbf[:, k * 512: k * 512 + 512],
                    start=(k == 0), stop=(k == 15)), after=aft)
                aft = []
            e1 = p.op("vector", lambda v, brep_sb=brep_sb: v.tensor_add(qm[:, 0:512], psK[:, 0:512], brep_sb[:, :]),
                      after=[lastmm, btok, evl.get(mi - 1)] + first_extra)
            if do_elu:
                e2 = p.op("vector", lambda v: v.tensor_scalar(qm[:, 512:1024], qm[:, 0:512], 0.0, None, ALU.min), after=[e1])
                e3 = p.op("scalar", lambda s: s.activation(qm[:, 512:1024], qm[:, 512:1024], AF.Exp), after=[e2])
                e4 = p.op("vector", lambda v: v.tensor_scalar(qm[:, 0:512], qm[:, 0:512], 0.0, None, ALU.max), after=[e3])
                e5 = p.op("vector", lambda v, mi=mi, dst=dst: v.tensor_add(
                    dst[:, mi * 512:(mi + 1) * 512], qm[:, 0:512], qm[:, 512:1024]), after=[e4])
            else:
                e5 = p.op("vector", lambda v, mi=mi, dst=dst: v.tensor_copy(dst[:, mi * 512:(mi + 1) * 512], qm[:, 0:512]), after=[e1])
            evl[mi] = e5
        return [evl[mi] for mi in range(8)]

    qdone = [ev_last[7]]
    sk_toks = kv_proj(wk_bf, t_wk, bkr, tcons[1], True, sk_bf, qdone)
    v_toks = kv_proj(wv_bf, t_wv, bvr, tcons[2], False, v_bf, [sk_toks[-1]])

    kvm = []
    aft0 = sk_toks + v_toks + [tcons[4]]
    for kvh in range(4):
        for k in range(8):
            kvm.append(p.op("tensor", lambda t, kvh=kvh, k=k: t.matmul(
                psKV[:, kvh * 128:(kvh + 1) * 128],
                sk_bf[:, k * 512 + kvh * 128: k * 512 + kvh * 128 + 128],
                v_bf[:, k * 512 + kvh * 128: k * 512 + kvh * 128 + 128],
                start=(k == 0), stop=(k == 7)), after=(aft0 if not kvm else [])))
            kvm.append(p.op("tensor", lambda t, kvh=kvh, k=k: t.matmul(
                psKV[:, 512 + kvh: 513 + kvh],
                sk_bf[:, k * 512 + kvh * 128: k * 512 + kvh * 128 + 128],
                ones_c[:, :],
                start=(k == 0), stop=(k == 7)), after=[]))
    ee = p.op("vector", lambda v: v.tensor_copy(kvev[:, :], psKV[:, 0:516]), after=kvm)
    p.dma("sync", kvp_out, kvev[:, :], after=[ee])
    p.op("sync", lambda s: s.nop(), after=p.barrier_tokens())
    return p.realize()


def run_L5(x1_list, inputs):
    nc = build_L5()
    shared = dict(
        WqT=f32h(inputs["L1_attn_Wq"].T), WkT=f32h(inputs["L1_attn_Wk"].T), WvT=f32h(inputs["L1_attn_Wv"].T),
        bq_rep=f32h(np.broadcast_to(inputs["L1_attn_bq"], (128, DM))),
        bk_rep=f32h(np.broadcast_to(inputs["L1_attn_bk"], (128, 512))),
        bv_rep=f32h(np.broadcast_to(inputs["L1_attn_bv"], (128, 512))),
        ident=np.eye(128, dtype=np.float16), ones_col=np.ones((128, 1), dtype=np.float16),
    )
    in_maps = []
    for c in range(NCORE):
        m = dict(shared)
        m["x1_in"] = f32h(x1_list[c])
        in_maps.append(m)
    res = run_bass_kernel_spmd(nc, in_maps, core_ids=list(range(NCORE)))
    return [{k: np.asarray(v) for k, v in r.items()} for r in res.results]


def build_L6():
    p = Prog()
    sq_in = p.din("sq_in", [TOK, DM], F32)
    kvp_a = p.din("kvp_a", [128, 516], F32)
    kvp_b = p.din("kvp_b", [128, 516], F32)
    WoT = p.din("WoT", [DM, DM], F32)
    ident_in = p.din("ident", [128, 128], F16)
    ones_in = p.din("ones_row", [1, 128], F32)
    s1_out = p.dout("s1", [TOK, DM], F32)

    kva = p.sb("kva", [128, 516], F32)
    kvb = p.sb("kvb", [128, 516], F32)
    kvt_bf = p.sb("kvt_bf", [128, 512], F16)
    ksum_bf = p.sb("ksum_bf", [128, 4], F32)
    ksr2 = p.sb("ksr2", [1, 512], F32)
    kb = p.sb("kb", [128, 512], F32)
    ident = p.sb("ident6", [128, 128], F16)
    identf = p.sb("identf6", [128, 128], F32)
    psF = p.ps("psF6", [128, 128])
    ones_r = p.sb("ones6", [1, 128], F32)
    wo_bf = p.sb("wo_bf6", [128, 16 * DM], F16)
    sqt = [p.sb(f"sqt{i}", [128, DM], F32) for i in range(2)]
    rn = p.sb("rn", [128, 16], F32)
    prod = p.sb("prod", [128, DM], F32)
    sqp = p.sb("sqp", [128, 8 * DM], F16)
    sqpT = p.sb("sqpT", [128, 16 * 1024], F16)
    att = sqp[:, :]
    s1e = [p.sb(f"s1e{i}", [128, DM], F32) for i in range(2)]

    psA = p.ps("psA6", [128, 1024])
    psA2 = p.ps("psA62", [128, 1024])
    psB = p.ps("psB6", [128, 512])
    psT = [p.ps(f"psT6{i}", [128, 1024], F16) for i in range(2)]

    identf_in = p.din("identf", [128, 128], F32)
    tc = [
        p.dma("sync", kva[:, :], kvp_a),
        p.dma("sync", kvb[:, :], kvp_b),
        p.dma("sync", ident[:, :], ident_in),
        p.dma("sync", ones_r[:, :], ones_in),
        p.dma("sync", identf[:, :], identf_in),
    ]
    t_wo = p.dma("gpsimd", wo_bf[:, :].rearrange("p (T n) -> p T n", T=16), WoT.rearrange("(T p) n -> p T n", p=128))

    a1 = p.op("vector", lambda v: v.tensor_add(kva[:, :], kva[:, :], kvb[:, :]), after=tc[0:2])
    a2 = p.op("vector", lambda v: v.tensor_copy(kvt_bf[:, :], kva[:, 0:512]), after=[a1])
    a3 = p.op("vector", lambda v: v.tensor_copy(ksum_bf[:, :], kva[:, 512:516]), after=[a1])
    tks = p.op("tensor", lambda t: t.transpose(psF[0:4, 0:128], ksum_bf[:, :], identf[:, :]), after=[a3, tc[4]])
    ksr4 = p.sb("ksr4", [4, 128], F32)
    ke2 = p.op("vector", lambda v: v.tensor_copy(ksr4[:, :], psF[0:4, 0:128]), after=[tks])
    kd = p.dma("sync", ksr2[0:1, :].rearrange("p (g x) -> p g x", g=4), ksr4[:, :], after=[ke2])
    kbm = []
    for kvh in range(4):
        kbm.append(p.op("tensor", lambda t, kvh=kvh: t.matmul(
            psB[:, kvh * 128:(kvh + 1) * 128], ones_r[:, :], ksr2[0:1, kvh * 128:(kvh + 1) * 128],
            start=True, stop=True), after=([kd, tc[3]] if kvh == 0 else [])))
    kbe = p.op("vector", lambda v: v.tensor_copy(kb[:, :], psB[:, 0:512]), after=kbm)

    kbv = kb[:, :].rearrange("p (g x) -> p g x", g=4).unsqueeze(2).broadcast_to([128, 4, 4, 128])
    sp_toks = []
    prev_n = None
    for ti in range(8):
        sb_ = sqt[ti % 2]
        tl = p.dma("sync", sb_[:, :], sq_in[ti * 128:(ti + 1) * 128, :], after=[sp_toks[ti - 2] if ti >= 2 else None])
        n1 = p.op("vector", lambda v, sb_=sb_, kbv=kbv: v.tensor_tensor(
            prod[:, :].rearrange("p (g h x) -> p g h x", g=4, h=4), sb_[:, :].rearrange("p (g h x) -> p g h x", g=4, h=4),
            kbv, ALU.mult), after=[tl, kbe, prev_n])
        n2 = p.op("vector", lambda v: v.tensor_reduce(rn[:, 0:16], prod[:, :].rearrange("p (g x) -> p g x", g=16), AX.X, ALU.add), after=[n1])
        n3 = p.op("vector", lambda v: v.tensor_scalar(rn[:, 0:16], rn[:, 0:16], 1.0 / 65536, 1e-6 / 65536, ALU.mult, ALU.add), after=[n2])
        n4 = p.op("vector", lambda v: v.reciprocal(rn[:, 0:16], rn[:, 0:16]), after=[n3])
        rnv = rn[:, 0:16].unsqueeze(2).broadcast_to([128, 16, 128])
        n5 = p.op("vector", lambda v, ti=ti, sb_=sb_, rnv=rnv: v.tensor_tensor(
            sqp[:, ti * DM:(ti + 1) * DM].rearrange("p (g x) -> p g x", g=16),
            sb_[:, :].rearrange("p (g x) -> p g x", g=16), rnv, ALU.mult), after=[n4])
        prev_n = n5
        sp_toks.append(n5)
    tr_toks = []
    ps_evac = {}
    idx = 0
    for t in range(8):
        for j in range(16):
            pt = psT[idx % 2]
            war = ps_evac.get(idx - 2, ke2 if idx < 2 else None)
            tt = p.op("tensor", lambda te, t=t, j=j, pt=pt: te.transpose(
                pt[:, 0:128], sqp[:, t * DM + j * 128: t * DM + j * 128 + 128], ident[:, :]),
                after=[sp_toks[t], tc[2], war, kbm[-1], ke2])
            cp = p.op("vector", lambda v, t=t, j=j, pt=pt: v.tensor_copy(
                sqpT[:, j * 1024 + t * 128: j * 1024 + t * 128 + 128], pt[:, 0:128]), after=[tt])
            ps_evac[idx] = cp
            tr_toks.append(cp)
            idx += 1
    att_toks = []
    prev_ae = {}
    for h in range(16):
        pp = psA if h % 2 == 0 else psA2
        mm = None
        for nj in range(2):
            mm = p.op("tensor", lambda t, h=h, nj=nj, pp=pp: t.matmul(
                pp[:, nj * 512:(nj + 1) * 512], kvt_bf[:, (h // 4) * 128:(h // 4) * 128 + 128],
                sqpT[:, h * 1024 + nj * 512: h * 1024 + nj * 512 + 512],
                start=True, stop=True),
                after=((tr_toks + [a2, kbe]) if h == 0 and nj == 0 else [prev_ae.get(h - 2)]))
        ae = p.op("vector", lambda v, h=h, pp=pp: v.tensor_scalar(att[:, h * 1024:(h + 1) * 1024], pp[:, 0:1024], 1.0 / 65536, None, ALU.mult),
                  after=[mm] + (tr_toks if h < 2 else []))
        prev_ae[h] = ae
        att_toks.append(ae)
    ev_last = {}
    for mi in range(8):
        b = s1e[mi % 2]
        etoks = []
        for half in range(2):
            pp = psA if half == 0 else psA2
            aft = (att_toks + [t_wo]) if mi == 0 and half == 0 else [ev_last.get(mi - 1), prev_ae.get(14 + half)]
            lastmm = None
            for k in range(16):
                for nj in range(2):
                    lastmm = p.op("tensor", lambda t, k=k, mi=mi, nj=nj, half=half, pp=pp: t.matmul(
                        pp[:, nj * 512:(nj + 1) * 512],
                        att[:, k * 1024 + mi * 128: k * 1024 + mi * 128 + 128],
                        wo_bf[:, k * DM + half * 1024 + nj * 512: k * DM + half * 1024 + nj * 512 + 512],
                        start=(k == 0), stop=(k == 15)), after=aft)
                    aft = []
            e1 = p.op("vector", lambda v, b=b, half=half, pp=pp: v.tensor_copy(b[:, half * 1024:(half + 1) * 1024], pp[:, 0:1024]),
                      after=[lastmm, ev_last.get(mi - 2)])
            etoks.append(e1)
        st = p.dma("sync", s1_out[mi * 128:(mi + 1) * 128, :], b[:, :], after=etoks)
        ev_last[mi] = st
    p.op("sync", lambda s: s.nop(), after=p.barrier_tokens())
    return p.realize()


def run_L6(l5res, inputs):
    nc = build_L6()
    in_maps = []
    for c in range(NCORE):
        peer = c ^ 1
        in_maps.append(dict(
            sq_in=f32h(l5res[c]["sq"]),
            kvp_a=f32h(l5res[c]["kvp"]), kvp_b=f32h(l5res[peer]["kvp"]),
            WoT=f32h(inputs["L1_attn_Wo"].T),
            ident=np.eye(128, dtype=np.float16), ones_row=np.ones((1, 128), dtype=np.float32),
            identf=np.eye(128, dtype=np.float32),
        ))
    res = run_bass_kernel_spmd(nc, in_maps, core_ids=list(range(NCORE)))
    return [np.asarray(res.results[c]["s1"]) for c in range(NCORE)]


def build_L8():
    HB = 1024
    NB8 = 8192 // HB
    p = Prog(dma_scratch=1024)
    xn1T_in = p.din("xn1T", [DM, TOK], F16)
    WupT = p.din("WupT", [DM, 8192], F16)
    WdnT = p.din("WdnT", [8192, DM], F16)
    bup_cols = p.din("bup_cols", [128, 64], F32)
    Aup = p.din("Aup", [DM, 16], F32)
    Bup = p.din("Bup", [16, 8192], F32)
    Adn = p.din("Adn", [8192, 16], F32)
    Bdn = p.din("Bdn", [16, DM], F32)
    bdn_rep = p.din("bdn_rep", [128, DM], F32)
    s1_in = p.din("s1_in", [TOK, DM], F32)
    x1_in = p.din("x1_in", [TOK, DM], F32)
    out = p.dout("out", [TOK, DM], F32)

    xn1T = p.sb("xn1T8", [128, 16 * 1024], F16)
    wup = p.sb("wup8", [128, 16 * 1024], F16)
    wdn = p.sb("wdn8", [128, 8 * 2048], F16)
    x2b = p.sb("x2b", [128, 8 * 1024], F16)
    y2ac = p.sb("y2ac", [128, 8 * DM], F32)
    bupc = p.sb("bupc", [128, 64], F32)
    aup_f = p.sb("aup_f", [128, 16 * 16], F32)
    aup_bf = p.sb("aup_bf", [128, 16 * 16], F16)
    adn_bf = p.sb("adn_bf", [128, 64 * 16], F16)
    bdn_bf = p.sb("bdn_bf", [16, DM], F16)
    bup_bf = p.sb("bup_bf", [16, 8192], F16)
    bdnr = p.sb("bdnr", [128, DM], F32)
    t1u = p.sb("t1u", [16, 1024], F16)
    t1d = p.sb("t1d", [16, 1024], F16)
    y2f = y2ac[:, :]
    adn_f = y2f[:, 0:1024]
    bdn_f = y2f[0:16, 1024:3072]
    wupf = wup[:, :].bitcast(F32)
    xtile = wupf[:, 0:2048]
    stile = wupf[:, 2048:4096]

    psU = p.ps("psU", [128, 1024])
    psD = p.ps("psD8", [128, DM])
    psL = p.ps("psL", [16, 1024])

    t_xn = p.dma("sync", xn1T[:, :].rearrange("p (T n) -> p T n", T=16), xn1T_in.rearrange("(T p) n -> p T n", p=128))
    t_bup = p.dma("sync", bupc[:, :], bup_cols)
    t_bdnr = p.dma("sync", bdnr[:, :], bdn_rep)
    t_aup = p.dma("sync", aup_f[:, :].rearrange("p (T n) -> p T n", T=16), Aup.rearrange("(T p) n -> p T n", p=128))
    c_aup = p.op("vector", lambda v: v.tensor_copy(aup_bf[:, :], aup_f[:, :]), after=[t_aup])
    t_adn = p.dma("sync", adn_f.rearrange("p (T n) -> p T n", T=64), Adn.rearrange("(T p) n -> p T n", p=128))
    c_adn = p.op("vector", lambda v: v.tensor_copy(adn_bf[:, :], adn_f), after=[t_adn])
    t_bdn = p.dma("sync", bdn_f, Bdn)
    c_bdn = p.op("vector", lambda v: v.tensor_copy(bdn_bf[:, :], bdn_f), after=[t_bdn])
    bupf2 = wdn[:, :].bitcast(F32)[0:16, 0:8192]
    t_bupl = p.dma("sync", bupf2, Bup)
    c_bup = p.op("vector", lambda v: v.tensor_copy(bup_bf[:, :], bupf2), after=[t_bupl])

    um = []
    for k in range(16):
        um.append(p.op("tensor", lambda t, k=k: t.matmul(
            psL[:, 0:512], aup_bf[:, k * 16:(k + 1) * 16], xn1T[:, k * 1024: k * 1024 + 512],
            start=(k == 0), stop=(k == 15)), after=([c_aup, t_xn] if k == 0 else [])))
        um.append(p.op("tensor", lambda t, k=k: t.matmul(
            psL[:, 512:1024], aup_bf[:, k * 16:(k + 1) * 16], xn1T[:, k * 1024 + 512: k * 1024 + 1024],
            start=(k == 0), stop=(k == 15)), after=[]))
    ue = p.op("vector", lambda v: v.tensor_copy(t1u[:, :], psL[0:16, 0:1024]), after=um)

    yev = {}
    t1d_mm = []
    wload = {}
    for blk in range(NB8):
        wl_u = p.dma("scalar", wup[:, :].rearrange("p (T n) -> p T n", T=16),
                     WupT[:, blk * HB:(blk + 1) * HB].rearrange("(T p) n -> p T n", p=128),
                     after=[c_bup, wload.get(("u", blk - 1))] if blk == 0 else [wload.get(("ur", blk - 1))])
        wl_d = p.dma("scalar", wdn[:, :].rearrange("p (T n) -> p T n", T=8),
                     WdnT[blk * HB:(blk + 1) * HB, :].rearrange("(T p) n -> p T n", p=128),
                     after=[c_bup] if blk == 0 else [wload.get(("dr", blk - 1))])
        prev_ev = None
        x2toks = []
        for mi in range(8):
            aft = [t_xn, wl_u, yev.get((blk - 1, "last")), prev_ev]
            lastmm = None
            for k in range(16):
                for nj in range(2):
                    lastmm = p.op("tensor", lambda t, k=k, mi=mi, nj=nj: t.matmul(
                        psU[:, nj * 512:(nj + 1) * 512], wup[:, k * 1024 + mi * 128: k * 1024 + mi * 128 + 128],
                        xn1T[:, k * 1024 + nj * 512: k * 1024 + nj * 512 + 512],
                        start=(k == 0), stop=False), after=aft)
                    aft = []
            for nj in range(2):
                lastmm = p.op("tensor", lambda t, mi=mi, nj=nj, blk=blk: t.matmul(
                    psU[:, nj * 512:(nj + 1) * 512], bup_bf[:, blk * HB + mi * 128: blk * HB + mi * 128 + 128],
                    t1u[:, nj * 512:(nj + 1) * 512],
                    start=False, stop=True), after=[ue, c_bup] if mi == 0 and nj == 0 else [])
            ev = p.op("scalar", lambda s, mi=mi, blk=blk: s.activation(
                x2b[:, mi * 1024:(mi + 1) * 1024], psU[:, 0:1024], AF.Relu,
                bias=bupc[:, blk * 8 + mi: blk * 8 + mi + 1]), after=[lastmm, t_bup, yev.get((blk - 1, "x2", mi))])
            prev_ev = ev
            x2toks.append(ev)
        wload[("ur", blk)] = x2toks[-1]
        dla = []
        for k in range(8):
            dla.append(p.op("tensor", lambda t, k=k, blk=blk: t.matmul(
                psL[:, 0:512], adn_bf[:, (blk * 8 + k) * 16: (blk * 8 + k) * 16 + 16],
                x2b[:, k * 1024: k * 1024 + 512],
                start=(blk == 0 and k == 0), stop=(blk == NB8 - 1 and k == 7)),
                after=([c_adn, ue] + x2toks if k == 0 else [])))
            dla.append(p.op("tensor", lambda t, k=k, blk=blk: t.matmul(
                psL[:, 512:1024], adn_bf[:, (blk * 8 + k) * 16: (blk * 8 + k) * 16 + 16],
                x2b[:, k * 1024 + 512: k * 1024 + 1024],
                start=(blk == 0 and k == 0), stop=(blk == NB8 - 1 and k == 7)), after=[]))
        t1d_mm += dla
        dprev = None
        for ti in range(8):
            aft = x2toks + [wl_d, yev.get((blk, ti - 1)), yev.get((blk - 1, ti))]
            dlast = None
            for k2 in range(8):
                for nj in range(4):
                    dlast = p.op("tensor", lambda t, k2=k2, ti=ti, nj=nj: t.matmul(
                        psD[:, nj * 512:(nj + 1) * 512], x2b[:, k2 * 1024 + ti * 128: k2 * 1024 + ti * 128 + 128],
                        wdn[:, k2 * DM + nj * 512: k2 * DM + nj * 512 + 512],
                        start=(k2 == 0), stop=(k2 == 7)), after=aft)
                    aft = []
            if blk == 0:
                ev = p.op("vector", lambda v, ti=ti: v.tensor_copy(y2ac[:, ti * DM:(ti + 1) * DM], psD[:, :]), after=[dlast, c_adn, c_bdn])
            else:
                ev = p.op("vector", lambda v, ti=ti: v.tensor_add(y2ac[:, ti * DM:(ti + 1) * DM], y2ac[:, ti * DM:(ti + 1) * DM], psD[:, :]), after=[dlast, yev.get((blk - 1, ti))])
            yev[(blk, ti)] = ev
            dprev = dlast
        wload[("dr", blk)] = dprev
        yev[(blk, "last")] = dprev
        for mi in range(8):
            yev[(blk, "x2", mi)] = dprev

    de = p.op("vector", lambda v: v.tensor_copy(t1d[:, :], psL[0:16, 0:1024]), after=t1d_mm)
    stprev = {}
    for ti in range(8):
        yt = y2ac[:, ti * DM:(ti + 1) * DM]
        lm = []
        aft = [de, c_bdn, yev[(NB8 - 1, ti)], stprev.get(ti - 1)]
        for nj in range(4):
            lm.append(p.op("tensor", lambda t, ti=ti, nj=nj: t.matmul(
                psD[:, nj * 512:(nj + 1) * 512], t1d[:, ti * 128:(ti + 1) * 128],
                bdn_bf[:, nj * 512:(nj + 1) * 512],
                start=True, stop=True), after=aft))
            aft = []
        tx = p.dma("sync", xtile, x1_in[ti * 128:(ti + 1) * 128, :], after=[stprev.get(ti - 1), wload[("ur", NB8 - 1)]])
        ts_ = p.dma("sync", stile, s1_in[ti * 128:(ti + 1) * 128, :], after=[stprev.get(ti - 1), wload[("ur", NB8 - 1)]])
        f1 = p.op("vector", lambda v, yt=yt: v.tensor_add(yt, yt, psD[:, :]), after=lm + [stprev.get(ti - 1)])
        f2 = p.op("vector", lambda v, yt=yt: v.tensor_add(yt, yt, bdnr[:, :]), after=[f1, t_bdnr])
        f3 = p.op("scalar", lambda s, yt=yt: s.activation(yt, yt, AF.Silu), after=[f2])
        f4 = p.op("vector", lambda v, yt=yt: v.tensor_mul(stile, yt, stile), after=[f3, ts_])
        f5 = p.op("vector", lambda v: v.tensor_add(stile, stile, xtile), after=[f4, tx])
        st = p.dma("sync", out[ti * 128:(ti + 1) * 128, :], stile, after=[f5])
        stprev[ti] = st
    p.op("sync", lambda s: s.nop(), after=p.barrier_tokens())
    return p.realize()


def run_L8(deq, l5res, s1_list, x1_list, inputs):
    nc = build_L8()
    bup = inputs["L1_ffn_b_up"].astype(np.float32)
    shared = dict(
        WupT=deq["W_upT"], WdnT=deq["W_dnT"],
        bup_cols=f32h(bup.reshape(64, 128).T),
        Aup=f32h(inputs["L1_ffn_w_up_lora_a"]), Bup=f32h(inputs["L1_ffn_w_up_lora_b"]),
        Adn=f32h(inputs["L1_ffn_w_down_lora_a"]), Bdn=f32h(inputs["L1_ffn_w_down_lora_b"]),
        bdn_rep=f32h(np.broadcast_to(inputs["L1_ffn_b_down"], (128, DM))),
    )
    in_maps = []
    for c in range(NCORE):
        m = dict(shared)
        m["xn1T"] = np.ascontiguousarray(np.asarray(l5res[c]["xn1"]).T)
        m["s1_in"] = f32h(s1_list[c])
        m["x1_in"] = f32h(x1_list[c])
        in_maps.append(m)
    res = run_bass_kernel_spmd(nc, in_maps, core_ids=list(range(NCORE)))
    return [np.asarray(res.results[c]["out"]) for c in range(NCORE)]


def kernel(**inputs):
    inputs = {k: np.asarray(v) for k, v in inputs.items()}
    deq = run_L0a(inputs)
    l1res = run_L1(inputs)
    l1res = [{k: np.asarray(v) for k, v in r.items()} for r in l1res]
    o_full = run_L2(l1res, inputs)
    s_list = run_L3(o_full, l1res, inputs)
    x1_list = run_L4(deq, l1res, s_list, inputs)
    l5res = run_L5(x1_list, inputs)
    s1_list = run_L6(l5res, inputs)
    out_list = run_L8(deq, l5res, s1_list, x1_list, inputs)
    out = np.zeros((BATCH, SEQ, DM), dtype=np.float32)
    for c in range(NCORE):
        b, hf = c // 2, c % 2
        out[b, hf * TOK:(hf + 1) * TOK] = np.asarray(out_list[c])
    return out
